# version 03
# baseline (speedup 1.0000x reference)
import contextlib
import numpy as np
import ml_dtypes
import concourse.bass as bass
import concourse.mybir as mybir
from concourse.bass_utils import run_bass_kernel_spmd

F32 = mybir.dt.float32
BF16 = mybir.dt.bfloat16
AF = mybir.ActivationFunctionType
ALU = mybir.AluOpType
AX = mybir.AxisListType

S = 4096
C = 1024
NB = S // 128
NCH = 8
H = 16
HD = 64
K9 = 9
EPS = 1e-5
NCORES = 8


def build_nc(debug=False):
    nc = bass.Bass("TRN2", target_bir_lowering=False)

    xin = nc.dram_tensor("xin", [S, C], F32, kind="ExternalInput")
    wrT_h = nc.dram_tensor("wrT", [C, C], F32, kind="ExternalInput")
    wkT_h = nc.dram_tensor("wkT", [C, C], F32, kind="ExternalInput")
    wvT_h = nc.dram_tensor("wvT", [C, C], F32, kind="ExternalInput")
    cwq_h = nc.dram_tensor("cwq", [NCH, 128, 162], F32, kind="ExternalInput")
    mur_h = nc.dram_tensor("mur", [128, NCH], F32, kind="ExternalInput")
    muk_h = nc.dram_tensor("muk", [128, NCH], F32, kind="ExternalInput")
    idb_h = nc.dram_tensor("idbh", [128, 128], BF16, kind="ExternalInput")
    out_h = nc.dram_tensor("out", [S, C], F32, kind="ExternalOutput")

    kd = "ExternalOutput" if debug else "Internal"
    ln1tm = nc.dram_tensor("ln1tm", [S + 8, C], BF16, kind=kd)
    ln1fm = nc.dram_tensor("ln1fm", [C, S], BF16, kind=kd)
    qdr = nc.dram_tensor("qdr", [S + 8, 9 * 144], F32, kind=kd)
    wtd = nc.dram_tensor("wtd", [S, 144], BF16, kind=kd)
    x2d = nc.dram_tensor("x2d", [S, C], F32, kind=kd)
    ln3fm = nc.dram_tensor("ln3fm", [C, S + 1], BF16, kind=kd)
    xrd = nc.dram_tensor("xrd", [C, S], BF16, kind=kd)
    xkd = nc.dram_tensor("xkd", [C, S], BF16, kind=kd)
    ydbg = nc.dram_tensor("ydbg", [S, C], F32, kind=kd)
    kdbg = nc.dram_tensor("kdbg", [C, S], BF16, kind=kd)
    rdbg = nc.dram_tensor("rdbg", [S, C], F32, kind=kd)
    fdbg = nc.dram_tensor("fdbg", [S, C], F32, kind=kd)

    with contextlib.ExitStack() as st:
        def sb(name, shape, dt=F32):
            return st.enter_context(nc.sbuf_tensor(name, shape, dt))

        wr_sb = sb("wr_sb", [128, NCH * C], BF16)
        wk_sb = sb("wk_sb", [128, NCH * C], BF16)
        wv_sb = sb("wv_sb", [128, NCH * C], BF16)
        cwq_sb = sb("cwq_sb", [128, NCH * 162], BF16)
        mur_sb = sb("mur_sb", [128, NCH])
        muk_sb = sb("muk_sb", [128, NCH])
        idb = sb("idb", [128, 128], BF16)
        consts = sb("consts", [128, 4])

        xbuf = sb("xbuf", [128, C])
        lnb = sb("lnb", [128, C], BF16)
        fmst = sb("fmst", [128, C], BF16)
        junk = sb("junk", [128, C])
        sumx = sb("sumx", [128, 1])
        sumq = sb("sumq", [128, 1])
        mean = sb("mean", [128, 1])
        var = sb("var", [128, 1])
        rstd = sb("rstd", [128, 1])
        nbias = sb("nbias", [128, 1])
        tmpv = sb("tmpv", [128, 1])
        nmean = sb("nmean", [128, 1])
        fmwin = sb("fmwin", [128, NCH * 128], BF16)
        qstage = sb("qstage", [128, 9 * 144])
        acc = sb("acc", [128, C])
        tmp = sb("tmp", [128, C])
        mx = sb("mx", [128, H])
        zsum = sb("zsum", [128, H])
        wtb = sb("wtb", [128, 144], BF16)
        xk9 = sb("xk9", [128, 9 * C], BF16)
        x2b = sb("x2b", [128, C])
        fmc = sb("fmc", [128, S // 2 + 1], BF16)
        diff = sb("diff", [128, S // 2], BF16)
        xrb = sb("xrb", [128, S // 2], BF16)
        xkb = sb("xkb", [128, S // 2], BF16)
        xr_sb = sb("xr_sb", [128, NCH * 512], BF16)
        xk_sb = sb("xk_sb", [128, NCH * 512], BF16)
        kact = sb("kact", [128, NCH * 512], BF16)
        kkb = sb("kkb", [128, 512])
        r_sb = sb("r_sb", [128, 4 * C])
        ffn = sb("ffn", [128, 4 * C])
        x2t = sb("x2t", [128, 4 * C])
        outb = sb("outb", [128, C])

        psQa = st.enter_context(nc.psum_tensor("psQa", [128, 162], F32))
        psQb = st.enter_context(nc.psum_tensor("psQb", [128, 162], F32))
        psT = st.enter_context(nc.psum_tensor("psT", [128, C], BF16))
        psA = st.enter_context(nc.psum_tensor("psA", [128, 512], F32))
        psB = st.enter_context(nc.psum_tensor("psB", [128, 512], F32))

        dsm = st.enter_context(nc.semaphore("dsm"))
        gsm = st.enter_context(nc.semaphore("gsm"))
        vsm = st.enter_context(nc.semaphore("vsm"))
        ssm = st.enter_context(nc.semaphore("ssm"))
        tsm = st.enter_context(nc.semaphore("tsm"))

        nc.const_aps.register(F32, 0.0, consts[:, 0:1])
        nc.const_aps.register(F32, 1.0, consts[:, 1:2])
        nc.const_aps.register(F32, float(EPS), consts[:, 2:3])

        ct = dict(d=0, g=0, v=0, s=0, t=0)

        class Ops:
            def __init__(self):
                self.l = dict(sync=[], vector=[], scalar=[], tensor=[], gpsimd=[])

            def add(self, eng, fn):
                self.l[eng].append(fn)

            def emit(self):
                with nc.Block() as blk:
                    if self.l["sync"]:
                        @blk.sync
                        def _(e, L=self.l["sync"]):
                            for f in L:
                                f(e)
                    if self.l["vector"]:
                        @blk.vector
                        def _(e, L=self.l["vector"]):
                            for f in L:
                                f(e)
                    if self.l["scalar"]:
                        @blk.scalar
                        def _(e, L=self.l["scalar"]):
                            for f in L:
                                f(e)
                    if self.l["tensor"]:
                        @blk.tensor
                        def _(e, L=self.l["tensor"]):
                            for f in L:
                                f(e)
                    if self.l["gpsimd"]:
                        @blk.gpsimd
                        def _(e, L=self.l["gpsimd"]):
                            for f in L:
                                f(e)

        def dma(ops, dst, src, eng="sync", **kw):
            key = "d" if eng == "sync" else "g"
            sem = dsm if eng == "sync" else gsm
            ct[key] += 16
            tgt = ct[key]
            ops.add(eng, lambda e, a=dst, b=src, s2=sem, k2=kw: e.dma_start(a, b, **k2).then_inc(s2, 16))
            return tgt

        def dma_t(ops, dst, src):
            ct["d"] += 16
            tgt = ct["d"]
            ops.add("sync", lambda e, a=dst, b=src: e.dma_start(a, b, transpose=True).then_inc(dsm, 16))
            return tgt

        def wait(ops, eng, sem_key, val):
            sem = dict(d=dsm, g=gsm, v=vsm, s=ssm, t=tsm)[sem_key]
            ops.add(eng, lambda e, s2=sem, v2=val: e.wait_ge(s2, v2))

        def ev(ops, eng, fn):
            key = dict(vector="v", scalar="s", tensor="t", gpsimd="g")[eng]
            sem = dict(v=vsm, s=ssm, t=tsm, g=gsm)[key]
            inc = 16 if eng == "gpsimd" else 1
            ct[key] += inc
            ops.add(eng, lambda e, f=fn, s2=sem, i2=inc: f(e).then_inc(s2, i2))
            return ct[key]

        def op(ops, eng, fn):
            ops.add(eng, fn)

        o = Ops()
        op(o, "vector", lambda e: e.memset(consts[:, :], 0.0))
        op(o, "vector", lambda e: e.memset(consts[:, 1:2], 1.0))
        op(o, "vector", lambda e: e.memset(consts[:, 2:3], float(EPS)))
        zdone = ev(o, "vector", lambda e: e.memset(qstage[:, :], 0.0))
        zdone = ev(o, "vector", lambda e: e.memset(lnb[:, :], 0.0))
        d_id = dma(o, idb[:, :], idb_h.ap())
        wait(o, "sync", "v", zdone)
        dma(o, ln1tm.ap()[0:4, :], lnb[0:4, 0:C])
        dma(o, ln1tm.ap()[S + 4:S + 8, :], lnb[0:4, 0:C])
        dma(o, qdr.ap()[0:4, :], qstage[0:4, 0:1296])
        pad_done = dma(o, qdr.ap()[S + 4:S + 8, :], qstage[0:4, 0:1296])
        dpad = dma(o, ln3fm.ap().rearrange("(c p) s -> p c s", p=128)[:, :, 0:1],
                   lnb[:, 0:NCH][:, :, None], allow_slow_non_contiguous=True)
        cast_done = 0
        for wsrc, wdst in ((wrT_h, wr_sb), (wkT_h, wk_sb), (wvT_h, wv_sb)):
            for c in range(NCH):
                if cast_done:
                    wait(o, "sync", "g", cast_done)
                dl = dma(o, junk[:, :], wsrc.ap()[128 * c:128 * (c + 1), :])
                wait(o, "gpsimd", "d", dl)
                cast_done = dma(o, wdst[:, c * C:(c + 1) * C], junk[:, :], eng="gpsimd")
        wait(o, "sync", "d", pad_done)
        dl = dma(o, qstage[:, 0:NCH * 162].rearrange("p (c n) -> p c n", c=NCH),
                 cwq_h.ap().rearrange("c p n -> p c n"))
        wait(o, "gpsimd", "d", dl)
        dcw = dma(o, cwq_sb[:, :], qstage[:, 0:NCH * 162], eng="gpsimd")
        dmu = dma(o, mur_sb[:, :], mur_h.ap())
        dmu = dma(o, muk_sb[:, :], muk_h.ap())
        o.emit()

        def ln_stats_apply(o, src_ap, dst_ap, last_reader_wait=()):
            v1 = ev(o, "vector", lambda e, a=src_ap: e.reduce_sum(sumx[:, :], a, AX.X))
            s1 = ev(o, "scalar", lambda e, a=src_ap: e.activation(
                junk[:, :], a, AF.Square, accum_out=sumq[:, :]))
            wait(o, "vector", "s", s1)
            op(o, "vector", lambda e: e.tensor_scalar_mul(mean[:, :], sumx[:, :], 1.0 / C))
            op(o, "vector", lambda e: e.tensor_scalar_mul(nmean[:, :], sumx[:, :], -1.0 / C))
            op(o, "vector", lambda e: e.drain())
            op(o, "vector", lambda e: e.tensor_mul(tmpv[:, :], mean[:, :], mean[:, :]))
            op(o, "vector", lambda e: e.tensor_scalar_mul(var[:, :], sumq[:, :], 1.0 / C))
            op(o, "vector", lambda e: e.drain())
            v2 = ev(o, "vector", lambda e: e.tensor_sub(var[:, :], var[:, :], tmpv[:, :]))
            wait(o, "scalar", "v", v2)
            s2 = ev(o, "scalar", lambda e: e.activation(
                rstd[:, :], var[:, :], AF.Sqrt, bias=float(EPS), scale=1.0))
            wait(o, "vector", "s", s2)
            op(o, "vector", lambda e: e.reciprocal(rstd[:, :], rstd[:, :]))
            op(o, "vector", lambda e: e.drain())
            v3 = ev(o, "vector", lambda e: e.tensor_mul(nbias[:, :], nmean[:, :], rstd[:, :]))
            wait(o, "scalar", "v", v3)
            for k2, v2w in last_reader_wait:
                if v2w:
                    wait(o, "scalar", k2, v2w)
            s3 = ev(o, "scalar", lambda e, a=src_ap, b=dst_ap: e.activation(
                b, a, AF.Identity, bias=nbias[:, :], scale=rstd[:, :]))
            return s3

        o = Ops()
        fm_store = 0
        tr_done = 0
        ev_done = 0
        tm_store = 0
        for i in range(NB):
            if i > 0:
                wait(o, "sync", "s", ap_done)
            dx = dma(o, xbuf[:, :], xin.ap()[128 * i:128 * (i + 1), :])
            wait(o, "vector", "d", dx)
            wait(o, "scalar", "d", dx)
            ap_done = ln_stats_apply(
                o, xbuf[:, :], lnb[:, :],
                last_reader_wait=(("t", tr_done), ("d", tm_store)))
            wait(o, "sync", "s", ap_done)
            tm_store = dma(o, ln1tm.ap()[4 + 128 * i:4 + 128 * (i + 1), :], lnb[:, :])
            wait(o, "tensor", "s", ap_done)
            if i > 0:
                wait(o, "tensor", "v", ev_done)
            for c in range(NCH):
                fn = lambda e, c2=c: e.transpose(
                    psT[:, 128 * c2:128 * (c2 + 1)],
                    lnb[:, 128 * c2:128 * (c2 + 1)], idb[:, :])
                if c == NCH - 1:
                    tr_done = ev(o, "tensor", fn)
                else:
                    op(o, "tensor", fn)
            wait(o, "vector", "t", tr_done)
            if i > 0:
                wait(o, "vector", "d", fm_store)
            ev_done = ev(o, "vector", lambda e: e.tensor_copy(fmst[:, :], psT[:, :]))
            wait(o, "sync", "v", ev_done)
            fm_store = dma(
                o, ln1fm.ap().rearrange("(c p) s -> p c s", p=128)[:, :, 128 * i:128 * (i + 1)],
                fmst[:, :].rearrange("p (c s) -> p c s", c=NCH))
        o.emit()

        o = Ops()
        q_store = 0
        for i in range(NB):
            if i > 0:
                wait(o, "sync", "t", mm_done)
            dw = dma(o, fmwin[:, :].rearrange("p (c s) -> p c s", c=NCH),
                     ln1fm.ap().rearrange("(c p) s -> p c s", p=128)[:, :, 128 * i:128 * (i + 1)])
            wait(o, "tensor", "d", dw)
            if i > 0:
                wait(o, "tensor", "s", evq_done)
            evq_done = 0
            for c in range(NCH):
                ps = psQa if (c % 2 == 0) else psQb
                if c >= 2:
                    wait(o, "tensor", "s", evq_list[c - 2])
                fn = lambda e, c2=c, p2=ps: e.matmul(
                    p2[:, :], fmwin[:, 128 * c2:128 * (c2 + 1)],
                    cwq_sb[:, 162 * c2:162 * (c2 + 1)], start=True, stop=True)
                mmc = ev(o, "tensor", fn)
                if c == 0:
                    evq_list = []
                wait(o, "scalar", "t", mmc)
                if i > 0 and c == 0:
                    wait(o, "scalar", "d", q_store)
                evq = ev(o, "scalar", lambda e, c2=c, p2=ps: e.activation(
                    qstage[:, :].rearrange("p (t h j) -> p t h j", t=9, h=H)[:, :, 2 * c2:2 * c2 + 2, :],
                    p2[:, :].rearrange("p (t hl j) -> p t hl j", t=9, hl=2),
                    AF.Identity, bias=0.0, scale=1.0))
                evq_list.append(evq)
                evq_done = evq
                if c == NCH - 1:
                    mm_done = mmc
            wait(o, "sync", "s", evq_done)
            q_store = dma(o, qdr.ap()[4 + 128 * i:4 + 128 * (i + 1), :], qstage[:, :])
        o.emit()

        o = Ops()
        wt_store = 0
        for i in range(NB):
            if i > 0:
                wait(o, "sync", "v", add_done)
            dq = 0
            for t in range(K9):
                dq = dma(o, qstage[:, 144 * t:144 * (t + 1)],
                         qdr.ap()[128 * i + t:128 * i + t + 128, 144 * t:144 * (t + 1)])
            wait(o, "vector", "d", dq)
            op(o, "vector", lambda e: e.tensor_add(acc[:, 0:144], qstage[:, 0:144], qstage[:, 144:288]))
            for t in range(2, K9):
                fn = lambda e, t2=t: e.tensor_add(
                    acc[:, 0:144], acc[:, 0:144], qstage[:, 144 * t2:144 * (t2 + 1)])
                if t == K9 - 1:
                    add_done = ev(o, "vector", fn)
                else:
                    op(o, "vector", fn)
            op(o, "vector", lambda e: e.reduce_max(
                mx[:, :, None], acc[:, 0:144].rearrange("p (h j) -> p h j", h=H), AX.X))
            op(o, "vector", lambda e: e.drain())
            v1 = ev(o, "vector", lambda e: e.tensor_sub(
                acc[:, 0:144].rearrange("p (h j) -> p h j", h=H),
                acc[:, 0:144].rearrange("p (h j) -> p h j", h=H),
                mx[:, :, None].broadcast_to([128, H, 9])))
            wait(o, "scalar", "v", v1)
            s1 = ev(o, "scalar", lambda e: e.activation(
                acc[:, 0:144], acc[:, 0:144], AF.Exp, bias=0.0, scale=1.0))
            wait(o, "vector", "s", s1)
            op(o, "vector", lambda e: e.reduce_sum(
                zsum[:, :, None], acc[:, 0:144].rearrange("p (h j) -> p h j", h=H), AX.X))
            op(o, "vector", lambda e: e.drain())
            op(o, "vector", lambda e: e.reciprocal(zsum[:, :], zsum[:, :]))
            op(o, "vector", lambda e: e.drain())
            if i > 0:
                wait(o, "vector", "d", wt_store)
            v2 = ev(o, "vector", lambda e: e.tensor_mul(
                wtb[:, :].rearrange("p (h j) -> p h j", h=H),
                acc[:, 0:144].rearrange("p (h j) -> p h j", h=H),
                zsum[:, :, None].broadcast_to([128, H, 9])))
            wait(o, "sync", "v", v2)
            wt_store = dma(o, wtd.ap()[128 * i:128 * (i + 1), :], wtb[:, :])
        o.emit()

        o = Ops()
        x2_store = 0
        for i in range(NB):
            if i > 0:
                wait(o, "sync", "v", mul_done)
                wait(o, "sync", "v", res_done)
            dw = dma(o, wtb[:, :], wtd.ap()[128 * i:128 * (i + 1), :])
            dx = dma(o, xbuf[:, :], xin.ap()[128 * i:128 * (i + 1), :])
            dk = 0
            for k in range(K9):
                dk = dma(o, xk9[:, C * k:C * (k + 1)],
                         ln1tm.ap()[128 * i + k:128 * i + k + 128, :])
            wait(o, "vector", "d", dk)
            wait(o, "vector", "d", dw)
            if i > 0:
                wait(o, "vector", "d", ydmp_prev)
            for k in range(K9):
                src = xk9[:, C * k:C * (k + 1)].rearrange("p (h d) -> p h d", h=H)
                wslice = wtb[:, :].rearrange("p (h j) -> p h j", h=H)[:, :, k:k + 1]
                if k == 0:
                    op(o, "vector", lambda e, a=src, b=wslice: e.tensor_mul(
                        acc[:, :].rearrange("p (h d) -> p h d", h=H), a,
                        b.broadcast_to([128, H, HD])))
                else:
                    op(o, "vector", lambda e, a=src, b=wslice: e.tensor_mul(
                        tmp[:, :].rearrange("p (h d) -> p h d", h=H), a,
                        b.broadcast_to([128, H, HD])))
                    fn = lambda e: e.tensor_add(acc[:, :], acc[:, :], tmp[:, :])
                    if k == K9 - 1:
                        mul_done = ev(o, "vector", fn)
                    else:
                        op(o, "vector", fn)
            wait(o, "sync", "v", mul_done)
            ydmp = dma(o, ydbg.ap()[128 * i:128 * (i + 1), :], acc[:, :])
            wait(o, "scalar", "v", mul_done)
            v1 = ev(o, "vector", lambda e: e.reduce_sum(sumx[:, :], acc[:, :], AX.X))
            wait(o, "scalar", "v", v1)
            s1 = ev(o, "scalar", lambda e: e.activation(
                junk[:, :], acc[:, :], AF.Square, accum_out=sumq[:, :]))
            wait(o, "vector", "s", s1)
            op(o, "vector", lambda e: e.tensor_scalar_mul(mean[:, :], sumx[:, :], 1.0 / C))
            op(o, "vector", lambda e: e.tensor_scalar_mul(nmean[:, :], sumx[:, :], -1.0 / C))
            op(o, "vector", lambda e: e.drain())
            op(o, "vector", lambda e: e.tensor_mul(tmpv[:, :], mean[:, :], mean[:, :]))
            op(o, "vector", lambda e: e.tensor_scalar_mul(var[:, :], sumq[:, :], 1.0 / C))
            op(o, "vector", lambda e: e.drain())
            v2 = ev(o, "vector", lambda e: e.tensor_sub(var[:, :], var[:, :], tmpv[:, :]))
            wait(o, "scalar", "v", v2)
            s2 = ev(o, "scalar", lambda e: e.activation(
                rstd[:, :], var[:, :], AF.Sqrt, bias=float(EPS), scale=1.0))
            wait(o, "vector", "s", s2)
            op(o, "vector", lambda e: e.reciprocal(rstd[:, :], rstd[:, :]))
            op(o, "vector", lambda e: e.drain())
            v3 = ev(o, "vector", lambda e: e.tensor_mul(nbias[:, :], nmean[:, :], rstd[:, :]))
            wait(o, "scalar", "v", v3)
            s3 = ev(o, "scalar", lambda e: e.activation(
                tmp[:, :], acc[:, :], AF.Identity, bias=nbias[:, :], scale=rstd[:, :]))
            wait(o, "vector", "s", s3)
            res_done = ev(o, "vector", lambda e: e.tensor_add(x2b[:, :], tmp[:, :], xbuf[:, :]))
            wait(o, "sync", "v", res_done)
            x2_store = dma(o, x2d.ap()[128 * i:128 * (i + 1), :], x2b[:, :])
            ydmp_prev = ydmp
        o.emit()

        o = Ops()
        fm_store = 0
        tr_done = 0
        ev_done = 0
        for i in range(NB):
            if i > 0:
                wait(o, "sync", "s", ap_done)
            dx = dma(o, xbuf[:, :], x2d.ap()[128 * i:128 * (i + 1), :])
            wait(o, "vector", "d", dx)
            wait(o, "scalar", "d", dx)
            ap_done = ln_stats_apply(
                o, xbuf[:, :], lnb[:, :],
                last_reader_wait=(("t", tr_done),))
            wait(o, "tensor", "s", ap_done)
            if i > 0:
                wait(o, "tensor", "v", ev_done)
            for c in range(NCH):
                fn = lambda e, c2=c: e.transpose(
                    psT[:, 128 * c2:128 * (c2 + 1)],
                    lnb[:, 128 * c2:128 * (c2 + 1)], idb[:, :])
                if c == NCH - 1:
                    tr_done = ev(o, "tensor", fn)
                else:
                    op(o, "tensor", fn)
            wait(o, "vector", "t", tr_done)
            if i > 0:
                wait(o, "vector", "d", fm_store)
            ev_done = ev(o, "vector", lambda e: e.tensor_copy(fmst[:, :], psT[:, :]))
            wait(o, "sync", "v", ev_done)
            fm_store = dma(
                o, ln3fm.ap().rearrange("(c p) s -> p c s", p=128)[:, :, 1 + 128 * i:1 + 128 * (i + 1)],
                fmst[:, :].rearrange("p (c s) -> p c s", c=NCH))
        o.emit()

        o = Ops()
        st_done = 0
        HS = S // 2
        for c in range(NCH):
            for hh in range(2):
                if c > 0 or hh > 0:
                    wait(o, "sync", "v", xk_done)
                dc = dma(o, fmc[:, :],
                         ln3fm.ap().rearrange("(c p) s -> p c s", p=128)
                         [:, c, HS * hh:HS * hh + HS + 1])
                wait(o, "vector", "d", dc)
                if st_done:
                    wait(o, "vector", "d", st_done)
                op(o, "vector", lambda e: e.tensor_sub(
                    diff[:, :], fmc[:, 1:HS + 1], fmc[:, 0:HS]))
                xr_done = ev(o, "vector", lambda e, c2=c: e.scalar_tensor_tensor(
                    xrb[:, :], diff[:, :], mur_sb[:, c2:c2 + 1], fmc[:, 0:HS],
                    ALU.mult, ALU.add))
                xk_done = ev(o, "vector", lambda e, c2=c: e.scalar_tensor_tensor(
                    xkb[:, :], diff[:, :], muk_sb[:, c2:c2 + 1], fmc[:, 0:HS],
                    ALU.mult, ALU.add))
                wait(o, "sync", "v", xk_done)
                dma(o, xrd.ap()[128 * c:128 * (c + 1), HS * hh:HS * (hh + 1)], xrb[:, :])
                st_done = dma(o, xkd.ap()[128 * c:128 * (c + 1), HS * hh:HS * (hh + 1)], xkb[:, :])
        o.emit()

        o = Ops()
        out_store = 0
        for ti in range(8):
            s0 = 512 * ti
            if ti > 0:
                wait(o, "sync", "t", v_mm_last)
                wait(o, "sync", "v", ffn_done)
            dxr = dma(o, xr_sb[:, :],
                      xrd.ap().rearrange("(c p) s -> p c s", p=128)[:, :, s0:s0 + 512])
            dxk = dma(o, xk_sb[:, :],
                      xkd.ap().rearrange("(c p) s -> p c s", p=128)[:, :, s0:s0 + 512])
            dx2 = dma(o, x2t[:, :].rearrange("p (u c) -> p u c", u=4),
                      x2d.ap().rearrange("(u p) c -> p u c", p=128)[:, 4 * ti:4 * ti + 4, :])
            wait(o, "tensor", "d", dxk)
            if ti > 0:
                wait(o, "tensor", "v", prev_gt)
            for d in range(NCH):
                ps = psA if d % 2 == 0 else psB
                if d >= 2:
                    wait(o, "tensor", "v", kk_evs[d - 2])
                for c in range(NCH):
                    fn = lambda e, c2=c, d2=d, p2=ps: e.matmul(
                        p2[:, :], wk_sb[:, c2 * C + 128 * d2: c2 * C + 128 * (d2 + 1)],
                        xk_sb[:, 512 * c2:512 * (c2 + 1)],
                        start=(c2 == 0), stop=(c2 == NCH - 1))
                    if c == NCH - 1:
                        mmk = ev(o, "tensor", fn)
                    else:
                        op(o, "tensor", fn)
                wait(o, "vector", "t", mmk)
                if d == 0:
                    kk_evs = []
                    kact_evs = []
                if d > 0:
                    wait(o, "vector", "s", kact_evs[d - 1])
                vr = ev(o, "vector", lambda e, p2=ps: e.tensor_scalar_max(
                    kkb[:, :], p2[:, :], 0.0))
                kk_evs.append(vr)
                wait(o, "scalar", "v", vr)
                sq2 = ev(o, "scalar", lambda e, d2=d: e.activation(
                    kact[:, 512 * d2:512 * (d2 + 1)], kkb[:, :], AF.Square))
                kact_evs.append(sq2)
            wait(o, "sync", "s", kact_evs[NCH - 1])
            dma(o, kdbg.ap().rearrange("(dd p) s -> p dd s", p=128)[:, :, s0:s0 + 512],
                kact[:, :].rearrange("p (dd s) -> p dd s", dd=NCH))
            wait(o, "tensor", "d", dxr)
            r_evs = []
            for u in range(4):
                for hf in range(2):
                    ps = psA if (2 * u + hf) % 2 == 0 else psB
                    idx = 2 * u + hf
                    if idx == 0:
                        wait(o, "tensor", "v", kk_evs[NCH - 2])
                    if idx == 1:
                        wait(o, "tensor", "v", kk_evs[NCH - 1])
                    if idx >= 2:
                        wait(o, "tensor", "s", r_evs[idx - 2])
                    for c in range(NCH):
                        fn = lambda e, c2=c, u2=u, h2=hf, p2=ps: e.matmul(
                            p2[:, :], xr_sb[:, 512 * c2 + 128 * u2: 512 * c2 + 128 * (u2 + 1)],
                            wr_sb[:, c2 * C + 512 * h2: c2 * C + 512 * (h2 + 1)],
                            start=(c2 == 0), stop=(c2 == NCH - 1))
                        if c == NCH - 1:
                            mmr = ev(o, "tensor", fn)
                        else:
                            op(o, "tensor", fn)
                    wait(o, "scalar", "t", mmr)
                    sg = ev(o, "scalar", lambda e, u2=u, h2=hf, p2=ps: e.activation(
                        r_sb[:, C * u2 + 512 * h2: C * u2 + 512 * (h2 + 1)],
                        p2[:, :], AF.Sigmoid, bias=0.0, scale=1.0))
                    r_evs.append(sg)
            wait(o, "sync", "s", r_evs[7])
            dma(o, rdbg.ap().rearrange("(u p) c -> p u c", p=128)[:, 4 * ti:4 * ti + 4, :],
                r_sb[:, :].rearrange("p (u c) -> p u c", u=4))
            for u in range(4):
                for hf in range(2):
                    ps = psA if (2 * u + hf) % 2 == 0 else psB
                    idx = 2 * u + hf
                    if idx == 0:
                        wait(o, "tensor", "s", r_evs[6])
                    if idx == 1:
                        wait(o, "tensor", "s", r_evs[7])
                    if idx >= 2:
                        wait(o, "tensor", "v", v_evs[idx - 2])
                    for d in range(NCH):
                        fn = lambda e, d2=d, u2=u, h2=hf, p2=ps: e.matmul(
                            p2[:, :], kact[:, 512 * d2 + 128 * u2: 512 * d2 + 128 * (u2 + 1)],
                            wv_sb[:, d2 * C + 512 * h2: d2 * C + 512 * (h2 + 1)],
                            start=(d2 == 0), stop=(d2 == NCH - 1))
                        if d == NCH - 1:
                            mmv = ev(o, "tensor", fn)
                        else:
                            op(o, "tensor", fn)
                    if idx == 0:
                        v_evs = []
                        wait(o, "vector", "s", r_evs[7])
                    wait(o, "vector", "t", mmv)
                    gt = ev(o, "vector", lambda e, u2=u, h2=hf, p2=ps: e.tensor_mul(
                        ffn[:, C * u2 + 512 * h2: C * u2 + 512 * (h2 + 1)],
                        p2[:, :], r_sb[:, C * u2 + 512 * h2: C * u2 + 512 * (h2 + 1)]))
                    v_evs.append(gt)
                    v_mm_last = mmv
                    prev_gt = gt
            wait(o, "sync", "v", v_evs[7])
            fdmp = dma(o, fdbg.ap().rearrange("(u p) c -> p u c", p=128)[:, 4 * ti:4 * ti + 4, :],
                       ffn[:, :].rearrange("p (u c) -> p u c", u=4))
            wait(o, "vector", "d", dx2)
            wait(o, "scalar", "d", dx2)
            for u in range(4):
                wait(o, "vector", "v", v_evs[2 * u + 1])
                wait(o, "scalar", "v", v_evs[2 * u + 1])
                fsl = ffn[:, C * u:C * (u + 1)]
                v1 = ev(o, "vector", lambda e, a=fsl: e.reduce_sum(sumx[:, :], a, AX.X))
                wait(o, "scalar", "v", v1)
                s1 = ev(o, "scalar", lambda e, a=fsl: e.activation(
                    junk[:, :], a, AF.Square, accum_out=sumq[:, :]))
                wait(o, "vector", "s", s1)
                op(o, "vector", lambda e: e.tensor_scalar_mul(mean[:, :], sumx[:, :], 1.0 / C))
                op(o, "vector", lambda e: e.tensor_scalar_mul(nmean[:, :], sumx[:, :], -1.0 / C))
                op(o, "vector", lambda e: e.drain())
                op(o, "vector", lambda e: e.tensor_mul(tmpv[:, :], mean[:, :], mean[:, :]))
                op(o, "vector", lambda e: e.tensor_scalar_mul(var[:, :], sumq[:, :], 1.0 / C))
                op(o, "vector", lambda e: e.drain())
                v2 = ev(o, "vector", lambda e: e.tensor_sub(var[:, :], var[:, :], tmpv[:, :]))
                wait(o, "scalar", "v", v2)
                s2 = ev(o, "scalar", lambda e: e.activation(
                    rstd[:, :], var[:, :], AF.Sqrt, bias=float(EPS), scale=1.0))
                wait(o, "vector", "s", s2)
                op(o, "vector", lambda e: e.reciprocal(rstd[:, :], rstd[:, :]))
                op(o, "vector", lambda e: e.drain())
                v3 = ev(o, "vector", lambda e: e.tensor_mul(nbias[:, :], nmean[:, :], rstd[:, :]))
                wait(o, "scalar", "v", v3)
                s3 = ev(o, "scalar", lambda e, a=fsl: e.activation(
                    tmp[:, :], a, AF.Identity, bias=nbias[:, :], scale=rstd[:, :]))
                wait(o, "vector", "s", s3)
                if ti > 0 or u > 0:
                    wait(o, "vector", "d", out_store)
                ffn_done = ev(o, "vector", lambda e, u2=u: e.tensor_add(
                    outb[:, :], tmp[:, :], x2t[:, C * u2:C * (u2 + 1)]))
                wait(o, "sync", "v", ffn_done)
                out_store = dma(o, out_h.ap()[s0 + 128 * u:s0 + 128 * (u + 1), :], outb[:, :])
        wait(o, "sync", "d", ct["d"])
        wait(o, "sync", "g", ct["g"])
        o.emit()

    return nc


_NC_CACHE = None


def _get_nc():
    global _NC_CACHE
    if _NC_CACHE is None:
        _NC_CACHE = build_nc()
    return _NC_CACHE


def kernel(L0_attn_query, L0_attn_conv_weight, L0_ffn_W_r, L0_ffn_W_k,
           L0_ffn_W_v, L0_ffn_mu_r, L0_ffn_mu_k):
    x = np.ascontiguousarray(np.asarray(L0_attn_query, dtype=np.float32))
    cw = np.asarray(L0_attn_conv_weight, dtype=np.float32)
    wr = np.asarray(L0_ffn_W_r, dtype=np.float32)
    wk = np.asarray(L0_ffn_W_k, dtype=np.float32)
    wv = np.asarray(L0_ffn_W_v, dtype=np.float32)
    mur = np.asarray(L0_ffn_mu_r, dtype=np.float32)
    muk = np.asarray(L0_ffn_mu_k, dtype=np.float32)

    wrT = np.ascontiguousarray(wr.T)
    wkT = np.ascontiguousarray(wk.T)
    wvT = np.ascontiguousarray(wv.T)
    cwq = np.zeros((NCH, 128, 162), dtype=np.float32)
    cwr = cw.reshape(H, 9, HD, K9)
    for cc in range(NCH):
        for hl in range(2):
            h = 2 * cc + hl
            blk = np.transpose(cwr[h], (1, 2, 0))
            cols = (np.arange(K9)[:, None] * 18 + hl * 9 + np.arange(9)[None, :])
            cwq[cc, 64 * hl:64 * (hl + 1), cols.reshape(-1)] = blk.reshape(HD, 81).T
    mur8 = np.ascontiguousarray(mur.reshape(NCH, 128).T)
    muk8 = np.ascontiguousarray(muk.reshape(NCH, 128).T)
    idb = np.eye(128, dtype=np.float32).astype(ml_dtypes.bfloat16)

    nc = _get_nc()
    in_maps = []
    for b in range(NCORES):
        in_maps.append({
            "xin": np.ascontiguousarray(x[b]),
            "wrT": wrT, "wkT": wkT, "wvT": wvT,
            "cwq": cwq, "mur": mur8, "muk": muk8, "idbh": idb,
        })
    res = run_bass_kernel_spmd(nc, in_maps, core_ids=list(range(NCORES)))
    out = np.stack([np.asarray(r["out"]) for r in res.results], axis=0)
    return out.astype(np.float32)
